# version 22
# baseline (speedup 1.0000x reference)
import contextlib
import math

import ml_dtypes
import numpy as np

import concourse.bass as bass
import concourse.mybir as mybir
from concourse.bass_utils import run_bass_kernel_spmd

F32 = mybir.dt.float32
BF16 = mybir.dt.bfloat16
AF = mybir.ActivationFunctionType
ALU = mybir.AluOpType
AX = mybir.AxisListType
FP8 = mybir.dt.float8e4

D = 2048
FF = 8192
ND = D // 128
NF = FF // 128
NED = D // 512
NEF = FF // 512


class _Src:
    __slots__ = ("name", "sem", "count", "ops", "satisfied")

    def __init__(self, name):
        self.name = name
        self.sem = None
        self.count = 0
        self.ops = []
        self.satisfied = {}


class Build:
    def __init__(self, nc):
        self.nc = nc
        self.st = {n: _Src(n) for n in ("pe", "ve", "act", "pool", "sp")}
        self.slots = {}
        self.lastw = {}
        self.lastr = {}

    def slot(self, name):
        s = _Src(name)
        self.slots[name] = s
        return s

    def _deps_rd(self, keys):
        return [self.lastw[k] for k in keys if self.lastw.get(k) is not None]

    def _deps_wr(self, keys):
        deps = []
        for k in keys:
            t = self.lastw.get(k)
            if t is not None:
                deps.append(t)
            deps.extend(self.lastr.get(k, ()))
        return deps

    def _dedup(self, st, alldeps):
        waits = []
        for tok in alldeps:
            if tok is None:
                continue
            src, v = tok
            key = id(src)
            if st.satisfied.get(key, -1) >= v:
                continue
            st.satisfied[key] = v
            waits.append((src, v))
        return waits

    def emit(self, stn, fn, rd=(), wr=(), war=(), deps=(), inc=True):
        st = self.st[stn]
        alldeps = list(deps) + self._deps_rd(rd) + self._deps_wr(wr) + self._deps_wr(war)
        waits = self._dedup(st, alldeps)
        if inc:
            st.count += 1
        myc = st.count
        mysem = st

        def op(eng, _waits=tuple(waits), _fn=fn, _inc=inc):
            for src, v in _waits:
                eng.wait_ge(src.sem, v)
            ins = _fn(eng)
            if _inc:
                ins.then_inc(mysem.sem, 1)

        st.ops.append(op)
        if inc:
            tok = (st, myc)
            for k in rd:
                self.lastr.setdefault(k, []).append(tok)
            for k in wr:
                self.lastw[k] = tok
                self.lastr[k] = []
            return tok
        assert not wr
        return None

    def dma(self, stn, slot, dst, src, rd=(), wr=(), deps=(), **kw):
        st = self.st[stn]
        alldeps = list(deps) + self._deps_rd(rd) + self._deps_wr(wr)
        if slot.count:
            alldeps.append((slot, slot.count))
        waits = self._dedup(st, alldeps)
        slot.count += 16
        myc = slot.count

        def op(eng, _waits=tuple(waits), _dst=dst, _src=src, _kw=kw):
            for s, v in _waits:
                eng.wait_ge(s.sem, v)
            eng.dma_start(_dst, _src, **_kw).then_inc(slot.sem, 16)

        st.ops.append(op)
        tok = (slot, myc)
        for k in rd:
            self.lastr.setdefault(k, []).append(tok)
        for k in wr:
            self.lastw[k] = tok
            self.lastr[k] = []
        return tok

    def raw_wait(self, stn, tok):
        src, v = tok
        self.st[stn].ops.append(lambda eng, _s=src, _v=v: eng.wait_ge(_s.sem, _v))


def build_program(S, T, dbg=False):
    NBA = T // 128
    NBF = 2
    NSUB = NBA // NBF
    NT = S // T
    assert S % T == 0 and T % 128 == 0 and NBA == 4
    DEN = float(np.float32(math.sqrt(float(D)) / float(S)))

    nc = bass.Bass("TRN2", target_bir_lowering=False)

    x_in = nc.dram_tensor("x", [S, D], F32, kind="ExternalInput").ap()
    wqt = nc.dram_tensor("wqt", [D, D], F32, kind="ExternalInput").ap()
    wkt = nc.dram_tensor("wkt", [D, D], F32, kind="ExternalInput").ap()
    wvt = nc.dram_tensor("wvt", [D, D], F32, kind="ExternalInput").ap()
    wot = nc.dram_tensor("wot", [D, D], F32, kind="ExternalInput").ap()
    wfct = nc.dram_tensor("wfct", [D, FF], F32, kind="ExternalInput").ap()
    wpjt = nc.dram_tensor("wpjt", [FF, D], F32, kind="ExternalInput").ap()
    out = nc.dram_tensor("out", [S, D], F32, kind="ExternalOutput").ap()
    if dbg:
        dbg_q = nc.dram_tensor("dbg_q", [S, D], BF16, kind="ExternalOutput").ap()
        dbg_kv = nc.dram_tensor("dbg_kv", [S, D], BF16, kind="ExternalOutput").ap()
        dbg_vh = nc.dram_tensor("dbg_vh", [S, D], BF16, kind="ExternalOutput").ap()
        dbg_x1 = nc.dram_tensor("dbg_x1", [S, D], F32, kind="ExternalOutput").ap()
        dbg_h = nc.dram_tensor("dbg_h", [S, FF], BF16, kind="ExternalOutput").ap()

    wq_b4 = nc.dram_tensor("wq_b4", [NED, 128, ND * 512], BF16).ap()
    wk_b4 = nc.dram_tensor("wk_b4", [NED, 128, ND * 512], BF16).ap()
    wv_b4 = nc.dram_tensor("wv_b4", [NED, 128, ND * 512], BF16).ap()
    wo_b4 = nc.dram_tensor("wo_b4", [NED, 128, ND * 512], BF16).ap()
    wfc_b4 = nc.dram_tensor("wfc_b4", [NEF, 128, ND * 512], FP8).ap()
    wpj_b4 = nc.dram_tensor("wpj_b4", [2, NF // 4, 128, 4 * 1024], FP8).ap()

    d_id = nc.inline_tensor(np.eye(128, dtype=ml_dtypes.bfloat16), name="c_id").ap()
    d_u = nc.inline_tensor(
        np.triu(np.ones((128, 128), np.float32)).astype(ml_dtypes.bfloat16),
        name="c_u").ap()
    d_a1 = nc.inline_tensor(np.ones((128, 128), dtype=ml_dtypes.bfloat16),
                            name="c_a1").ap()
    d_i32 = nc.inline_tensor(np.eye(128, dtype=np.float32), name="c_i32").ap()
    d_oc = nc.inline_tensor(np.ones((128, 1), dtype=np.float32), name="c_oc").ap()
    d_or = nc.inline_tensor(np.ones((1, 128), dtype=np.float32), name="c_or").ap()

    with contextlib.ExitStack() as es:
        E = es.enter_context
        xb = E(nc.sbuf_tensor("xb", [128, NBA * D], F32))
        xn = E(nc.sbuf_tensor("xn", [128, 2 * D], BF16))
        xnT = E(nc.sbuf_tensor("xnT", [128, NBA * D], BF16))
        qv = E(nc.sbuf_tensor("qv", [128, NBA * D], BF16))
        kv = E(nc.sbuf_tensor("kv", [128, NBA * D], BF16))
        h = E(nc.sbuf_tensor("h", [128, NBF * FF], BF16))
        hqT = E(nc.sbuf_tensor("hqT", [128, NBF * 2 * 1024], BF16))
        carry = E(nc.sbuf_tensor("carry", [128, D], F32))
        wat = E(nc.sbuf_tensor("wat", [128, 3 * ND * 512], BF16))
        wpj = E(nc.sbuf_tensor("wpj", [128, 2 * 4096], BF16))
        c_id = E(nc.sbuf_tensor("cid", [128, 128], BF16))
        c_u = E(nc.sbuf_tensor("cu", [128, 128], BF16))
        c_a1 = E(nc.sbuf_tensor("ca1", [128, 128], BF16))
        c_i32 = E(nc.sbuf_tensor("ci32", [128, 128], F32))
        c_oc = E(nc.sbuf_tensor("coc", [128, 1], F32))
        c_or = E(nc.sbuf_tensor("cor", [1, 128], F32))
        stA = E(nc.sbuf_tensor("stA", [128, NBA * 24], F32))
        stH = E(nc.sbuf_tensor("stH", [128, NBF * 96], F32))
        mv1 = E(nc.sbuf_tensor("mv1", [128, NBA * 2], F32))
        mv2 = E(nc.sbuf_tensor("mv2", [128, NBA * 2], F32))
        mv3 = E(nc.sbuf_tensor("mv3", [128, NBF * 2], F32))
        r1 = E(nc.sbuf_tensor("r1", [128, NBA], F32))
        r2 = E(nc.sbuf_tensor("r2", [128, NBA], F32))
        r3 = E(nc.sbuf_tensor("r3", [128, NBF], F32))
        nm1 = E(nc.sbuf_tensor("nm1", [128, NBA], F32))
        nm2 = E(nc.sbuf_tensor("nm2", [128, NBA], F32))
        nm3 = E(nc.sbuf_tensor("nm3", [128, NBF], F32))
        thgf = E(nc.sbuf_tensor("thgf", [128, 4], F32))
        thgp = E(nc.sbuf_tensor("thgp", [128, 4], F32))
        gparts = E(nc.sbuf_tensor("gparts", [128, 64], F32))
        gsum = E(nc.sbuf_tensor("gsum", [128, 1], F32))
        gtmp = E(nc.sbuf_tensor("gtmp", [1, 4], F32))

        pt = [E(nc.psum_tensor(f"pt{j}", [128, 1024], BF16)) for j in range(2)]
        pa = [E(nc.psum_tensor(f"pa{j}", [128, 512], F32)) for j in range(4)]
        pc = [E(nc.psum_tensor(f"pc{j}", [128, 512], F32)) for j in range(2)]

        bd = Build(nc)
        for n, st in bd.st.items():
            st.sem = E(nc.semaphore(f"se_{n}"))
        s_const = bd.slot("const")
        s_x = [bd.slot(f"x{j}") for j in range(2)]
        s_wat = [bd.slot(f"wat{j}") for j in range(3)]
        s_wfc = [bd.slot(f"wfc{j}") for j in range(3)]
        s_wpj = [bd.slot(f"wpj{j}") for j in range(2)]
        s_out = [bd.slot(f"out{j}") for j in range(2)]
        s_cast = bd.slot("cast")
        s_p0i = [bd.slot(f"p0i{j}") for j in range(2)]
        s_p0o = [bd.slot(f"p0o{j}") for j in range(2)]
        s_dbg = bd.slot("dbg")
        for s in bd.slots.values():
            s.sem = E(nc.semaphore(f"sd_{s.name}"))

        emit, dma = bd.emit, bd.dma

        t_c = None
        for dst, src in ((c_id, d_id), (c_u, d_u), (c_a1, d_a1),
                         (c_i32, d_i32), (c_oc, d_oc), (c_or, d_or)):
            t_c = dma("sp", s_const, dst[:], src)
        emit("ve", lambda e: e.memzero(carry[:]),
             wr=[("carry", ee) for ee in range(NED)])

        cast_toks = {}
        prev = None
        for nm, dst4, srct in (("k", wk_b4, wkt), ("v", wv_b4, wvt),
                               ("q", wq_b4, wqt), ("o", wo_b4, wot)):
            src4 = srct.rearrange("(d p) (e j) -> e p d j", p=128, j=512)
            for e4 in range(NED):
                prev = dma("pool", s_cast,
                           dst4[e4].rearrange("p (d j) -> p d j", j=512),
                           src4[e4], deps=[prev] if prev else [])
            cast_toks[nm] = prev

        def p0in(sl):
            return h[:, FF + sl * 4096: FF + (sl + 1) * 4096].bitcast(F32)

        def p0in_hkeys(sl):
            return [("h", 1, cc) for cc in range(8 * sl, 8 * sl + 8)]

        def p0out(sl):
            return h[:, sl * 2048:(sl + 1) * 2048]

        def p0out_hkeys(sl):
            return [("h", 0, cc) for cc in range(4 * sl, 4 * sl + 4)]

        def p0eng(j):
            return "ve"

        def phase0_matrix(w_in, nrows, ncols, thg, tag, outdst):
            nchunk_c = ncols // 2048
            nchunks = (nrows // 128) * nchunk_c
            assert nchunks <= 64
            wr_src = w_in.rearrange("(r p) c -> r p c", p=128)
            for j in range(nchunks):
                r, c = divmod(j, nchunk_c)
                sl = j % 2
                ti = dma("sp", s_p0i[sl], p0in(sl),
                         wr_src[r, :, c * 2048:(c + 1) * 2048],
                         wr=[(tag + "p0i", sl)] + p0in_hkeys(sl))
                emit("act", lambda e, _s=sl, _j=j: e.activation(
                    p0out(3), p0in(_s), AF.Abs,
                    accum_out=gparts[:, _j:_j + 1]),
                    rd=[(tag + "p0i", sl)] + p0in_hkeys(sl),
                    wr=[("gparts", j), ("p0o", 3)] + p0out_hkeys(3), deps=[ti])
            t_gs = emit("ve", lambda e, _n=nchunks: e.tensor_reduce(
                gsum[:], gparts[:, 0:_n], AX.X, ALU.add),
                rd=[("gparts", jj) for jj in range(nchunks)], wr=[("gsum",)])
            t_pm = emit("pe", lambda e: e.matmul(
                pa[0][0:1, 0:1], c_oc[:], gsum[:], start=True, stop=True),
                rd=[("gsum",)], wr=[("pa", 0)], deps=[t_c, t_gs])
            nel = float(1.0 / (nrows * ncols))
            t1 = emit("ve", lambda e, _ne=nel: e.tensor_single_scalar(
                gtmp[0:1, 2:3], pa[0][0:1, 0:1], _ne, ALU.mult),
                rd=[("pa", 0)], wr=[("gtmp2",)], deps=[t_pm])
            t2 = emit("ve", lambda e: e.tensor_scalar(
                gtmp[0:1, 0:1], gtmp[0:1, 2:3], 1e-6, 0.5,
                ALU.add, ALU.mult), rd=[("gtmp2",)], wr=[("gtmp0",)], deps=[t1])
            t3 = emit("ve", lambda e: e.tensor_single_scalar(
                gtmp[0:1, 1:2], gtmp[0:1, 0:1], -1.0, ALU.mult),
                rd=[("gtmp0",)], wr=[("gtmp1",)], deps=[t2])
            t4 = emit("ve", lambda e: e.tensor_single_scalar(
                gtmp[0:1, 3:4], gtmp[0:1, 2:3], 0.5, ALU.mult),
                rd=[("gtmp2",)], wr=[("gtmp3",)], deps=[t1])
            t_bc = emit("pe", lambda e: e.matmul(
                pa[1][0:128, 0:4], c_or[:], gtmp[0:1, 0:4], start=True, stop=True),
                rd=[("gtmp0",), ("gtmp1",), ("gtmp2",), ("gtmp3",)], wr=[("pa", 1)],
                deps=[t1, t2, t3, t4, t_c])
            t_thg = emit("ve", lambda e, _t=thg: e.tensor_copy(_t[:, 0:4], pa[1][:, 0:4]),
                         rd=[("pa", 1)], wr=[(tag + "thg",)], deps=[t_bc])
            for j in range(nchunks):
                r, c = divmod(j, nchunk_c)
                sl = j % 2
                oa, ob = 2 * sl, 2 * sl + 1
                eng = p0eng(j)
                ti = dma("sp", s_p0i[sl], p0in(sl),
                         wr_src[r, :, c * 2048:(c + 1) * 2048],
                         wr=[(tag + "p0i", sl)] + p0in_hkeys(sl))
                tA = emit("act", lambda e, _s=sl, _o=oa, _t=thg: e.activation(
                    p0out(_o), p0in(_s), AF.Sign, bias=_t[:, 1:2]),
                    rd=[(tag + "p0i", sl), (tag + "thg",)] + p0in_hkeys(sl),
                    wr=[("p0o", oa)] + p0out_hkeys(oa), deps=[ti, t_thg])
                tB = emit("act", lambda e, _s=sl, _o=ob, _t=thg: e.activation(
                    p0out(_o), p0in(_s), AF.Sign, bias=_t[:, 0:1]),
                    rd=[(tag + "p0i", sl)] + p0in_hkeys(sl),
                    wr=[("p0o", ob)] + p0out_hkeys(ob), deps=[ti])
                tC = emit("ve", lambda e, _o=oa: e.tensor_tensor(
                    p0out(_o), p0out(_o), p0out(_o + 1), ALU.add),
                    rd=[("p0o", ob)] + p0out_hkeys(ob),
                    wr=[("p0o", oa)] + p0out_hkeys(oa), deps=[tA, tB])
                dst, src_re = outdst(r, c, p0out(oa))
                dma("pool", s_p0o[sl], dst, src_re,
                    rd=[("p0o", oa)] + p0out_hkeys(oa), deps=[tC])

        def fc_outdst(r, c, srcap):
            dst = wfc_b4[4 * c:4 * c + 4, :, r * 512:(r + 1) * 512]
            return (dst.rearrange("e p j -> p e j"),
                    srcap.rearrange("p (e j) -> p e j", j=512))

        def pj_outdst(r, c, srcap):
            g4, dd = divmod(r, 4)
            dst = wpj_b4[0:2, g4, :, dd * 1024:(dd + 1) * 1024]
            return (dst.rearrange("s p j -> p s j"),
                    srcap.rearrange("p (s j) -> p s j", j=1024))

        def xsl(i, b, e4=None):
            base = b * D
            if e4 is None:
                return xb[:, base:base + D]
            return xb[:, base + e4 * 512: base + (e4 + 1) * 512]

        xqT = xnT

        def wat_view(sl):
            return wat[:, sl * ND * 512:(sl + 1) * ND * 512].rearrange(
                "p (n w) -> p n w", n=ND)



        def transpose_block(src_fn, dst_fn, src_keys, dst_key_fn, extra_deps):
            toks = {}
            for half in range(2):
                tt = None
                for d8 in range(8):
                    d = half * 8 + d8
                    tt = emit("pe", lambda e, _h=half, _d8=d8, _d=d: e.transpose(
                        pt[_h][:, _d8 * 128:(_d8 + 1) * 128], src_fn(_d), c_id[:]),
                        rd=src_keys if d8 in (0, 7) else (),
                        war=[("pt", half)] if d8 == 0 else (),
                        wr=[("pt", half)] if d8 == 7 else (),
                        deps=(list(extra_deps) + [t_c]) if d8 == 0 else (),
                        inc=(d8 == 7))
                toks[half] = emit("ve", lambda e, _h=half: e.tensor_copy(
                    dst_fn(_h), pt[_h][:, :]),
                    rd=[("pt", half)], wr=dst_key_fn(half), deps=[tt])
            return toks

        wat_ring = [0]
        pa_ring = [0]

        def next_wat():
            sl = wat_ring[0] % 3
            wat_ring[0] += 1
            return sl

        def next_pa():
            bk = pa_ring[0] % 6
            pa_ring[0] += 1
            return bk

        def pa6_ap(k):
            return pa[k][:, :] if k < 4 else pc[k - 4][:, :]

        def pa6_key(k):
            return ("pa", k) if k < 4 else ("pc", k - 4)

        def tile_load(i):
            toks = []
            for b in range(NBA):
                toks.append(dma("sp", s_x[b % 2],
                                xb[:, b * D:(b + 1) * D],
                                x_in[(i * NBA + b) * 128:(i * NBA + b + 1) * 128, :],
                                wr=[("x", b, ee) for ee in range(NED)] + [("xf", b)]))
            return toks

        def phase_A(i, t_xin):
            t_xnT = {}
            for b in range(NBA):
                pb = b % 2
                for c in range(4):
                    emit("ve", lambda e, _b=b, _c=c, _i=i: e.bn_stats(
                        stA[:, _b * 24 + _c * 6:_b * 24 + (_c + 1) * 6],
                        xsl(_i, _b, _c)),
                        rd=[("x", b, c)], wr=[("stA", b, c)], deps=[t_xin[b]])
                t_mv = emit("ve", lambda e, _b=b: e.bn_aggr(
                    mv1[:, 2 * _b:2 * _b + 2], stA[:, _b * 24:(_b + 1) * 24]),
                    rd=[("stA", b, cc) for cc in range(4)], wr=[("mv1", b)])
                t_sq = emit("act", lambda e, _b=b: e.activation(
                    r1[:, _b:_b + 1], mv1[:, 2 * _b + 1:2 * _b + 2], AF.Sqrt,
                    bias=np.float32(1e-5)),
                    rd=[("mv1", b)], wr=[("r1", b)], deps=[t_mv])
                t_r = emit("ve", lambda e, _b=b: e.reciprocal(
                    r1[:, _b:_b + 1], r1[:, _b:_b + 1]), wr=[("r1", b)], deps=[t_sq])
                t_nm = emit("ve", lambda e, _b=b: e.tensor_scalar(
                    nm1[:, _b:_b + 1], mv1[:, 2 * _b:2 * _b + 1], r1[:, _b:_b + 1],
                    -1.0, ALU.mult, ALU.mult),
                    rd=[("mv1", b), ("r1", b)], wr=[("nm1", b)], deps=[t_r])
                t_xn = emit("act", lambda e, _b=b, _pb=pb, _i=i: e.activation(
                    xn[:, _pb * D:(_pb + 1) * D], xsl(_i, _b), AF.Identity,
                    bias=nm1[:, _b:_b + 1], scale=r1[:, _b:_b + 1]),
                    rd=[("r1", b), ("nm1", b)] + [("x", b, ee) for ee in range(NED)],
                    wr=[("xa", pb)], deps=[t_nm, t_xin[b]])
                t_xnT.update({(b, hh): tk for hh, tk in transpose_block(
                    lambda _d, _pb=pb: xn[:, _pb * D + _d * 128:_pb * D + (_d + 1) * 128],
                    lambda _h, _b=b: xnT[:, _b * D + _h * 1024:_b * D + (_h + 1) * 1024],
                    [("xa", pb)], lambda _h, _b=b: [("xT", _b, _h)],
                    [t_xn]).items()})
            return t_xnT

        def phase_B(i, t_xnT):
            t_q, t_kv = {}, {}
            for w4, cname in ((wk_b4, "k"), (wv_b4, "v"), (wq_b4, "q")):
                for e4 in range(NED):
                    sl = next_wat()
                    t_w = dma("sp", s_wat[sl], wat[:, sl * ND * 512:(sl + 1) * ND * 512],
                              w4[e4], wr=[("wat", sl)], deps=[cast_toks[cname]])
                    for b in range(NBA):
                        bank = next_pa()
                        tm = None
                        for d in range(ND):
                            tm = emit("pe", lambda e, _sl=sl, _b=b, _d=d, _bk=bank: e.matmul(
                                pa6_ap(_bk),
                                xnT[:, _b * D + _d * 128:_b * D + (_d + 1) * 128],
                                wat_view(_sl)[:, _d, :],
                                start=(_d == 0), stop=(_d == ND - 1)),
                                rd=[("wat", sl), ("xT", b, 0), ("xT", b, 1)] if d in (0, ND - 1) else (),
                                war=[pa6_key(bank)] if d == 0 else (),
                                wr=[pa6_key(bank)] if d == ND - 1 else (),
                                deps=[t_w, t_xnT[(b, 0)], t_xnT[(b, 1)]] if d == 0 else (),
                                inc=(d == ND - 1))
                        if cname == "k":
                            emit("ve", lambda e, _b=b, _e=e4, _bk=bank: e.tensor_copy(
                                qv[:, _b * D + _e * 512:_b * D + (_e + 1) * 512],
                                pa6_ap(_bk)),
                                rd=[pa6_key(bank)], wr=[("qv", b, e4)], deps=[tm])
                        elif cname == "v":
                            t_kv[(b, e4)] = emit("ve", lambda e, _b=b, _e=e4, _bk=bank: e.tensor_tensor(
                                kv[:, _b * D + _e * 512:_b * D + (_e + 1) * 512],
                                qv[:, _b * D + _e * 512:_b * D + (_e + 1) * 512],
                                pa6_ap(_bk), ALU.mult),
                                rd=[pa6_key(bank), ("qv", b, e4)],
                                wr=[("kv", b, e4)], deps=[tm])
                        else:
                            t_q[(b, e4)] = emit("ve", lambda e, _b=b, _e=e4, _bk=bank: e.tensor_copy(
                                qv[:, _b * D + _e * 512:_b * D + (_e + 1) * 512],
                                pa6_ap(_bk)),
                                rd=[pa6_key(bank)], wr=[("qv", b, e4)], deps=[tm])
            return t_q, t_kv

        def phase_C(i, t_q, t_kv):
            t_vhat = {}
            for e4 in range(NED):
                for b in range(NBA):
                    bank = b % 2
                    emit("pe", lambda e, _b=b, _e=e4, _bk=bank: e.matmul(
                        pc[_bk][:, :], c_u[:],
                        kv[:, _b * D + _e * 512:_b * D + (_e + 1) * 512],
                        start=True, stop=False),
                        rd=[("kv", b, e4)], war=[("pc", bank)],
                        deps=[t_kv[(b, e4)]], inc=False)
                    for ib in range(b):
                        emit("pe", lambda e, _ib=ib, _e=e4, _bk=bank: e.matmul(
                            pc[_bk][:, :], c_a1[:],
                            kv[:, _ib * D + _e * 512:_ib * D + (_e + 1) * 512],
                            start=False, stop=False),
                            deps=[t_kv[(ib, e4)]], inc=False)
                    t_kvt = emit("pe", lambda e, _e=e4, _bk=bank: e.matmul(
                        pc[_bk][:, :], c_i32[:],
                        carry[:, _e * 512:(_e + 1) * 512],
                        start=False, stop=True),
                        rd=[("carry", e4)], wr=[("pc", bank)])
                    t_vhat[(b, e4)] = emit("ve", lambda e, _b=b, _e=e4, _bk=bank: e.scalar_tensor_tensor(
                        qv[:, _b * D + _e * 512:_b * D + (_e + 1) * 512],
                        qv[:, _b * D + _e * 512:_b * D + (_e + 1) * 512],
                        DEN, pc[_bk][:, :], ALU.mult, ALU.mult),
                        rd=[("pc", bank)], wr=[("qv", b, e4)],
                        deps=[t_kvt, t_q[(b, e4)]])
                emit("pe", lambda e, _e=e4: e.matmul(
                    pa[3][:, :], c_i32[:], carry[:, _e * 512:(_e + 1) * 512],
                    start=True, stop=False),
                    rd=[("carry", e4)], war=[("pa", 3)], inc=False)
                tcu = None
                for b in range(NBA):
                    tcu = emit("pe", lambda e, _b=b, _e=e4: e.matmul(
                        pa[3][:, :], c_a1[:],
                        kv[:, _b * D + _e * 512:_b * D + (_e + 1) * 512],
                        start=False, stop=(_b == NBA - 1)),
                        rd=[("kv", b, e4)] if b == NBA - 1 else (),
                        wr=[("pa", 3)] if b == NBA - 1 else (),
                        inc=(b == NBA - 1))
                emit("ve", lambda e, _e=e4: e.tensor_copy(
                    carry[:, _e * 512:(_e + 1) * 512], pa[3][:, :]),
                    rd=[("pa", 3)], wr=[("carry", e4)], deps=[tcu])
            return t_vhat

        def phase_D(i, t_vhat):
            t_vT = {}
            for b in range(NBA):
                t_vT.update({(b, hh): tk for hh, tk in transpose_block(
                    lambda _d, _b=b: qv[:, _b * D + _d * 128:_b * D + (_d + 1) * 128],
                    lambda _h, _b=b: kv[:, _b * D + _h * 1024:_b * D + (_h + 1) * 1024],
                    [("qv", b, ee) for ee in range(NED)],
                    lambda _h, _b=b: [("kv", _b, 2 * _h), ("kv", _b, 2 * _h + 1)],
                    [t_vhat[(b, ee)] for ee in range(NED)]).items()})

            t_x1 = {}
            for e4 in range(NED):
                sl = next_wat()
                t_w = dma("sp", s_wat[sl], wat[:, sl * ND * 512:(sl + 1) * ND * 512],
                          wo_b4[e4], wr=[("wat", sl)], deps=[cast_toks["o"]])
                for b in range(NBA):
                    bank = next_pa()
                    tm = None
                    for d in range(ND):
                        tm = emit("pe", lambda e, _sl=sl, _b=b, _d=d, _bk=bank: e.matmul(
                            pa6_ap(_bk),
                            kv[:, _b * D + _d * 128:_b * D + (_d + 1) * 128],
                            wat_view(_sl)[:, _d, :],
                            start=(_d == 0), stop=(_d == ND - 1)),
                            rd=[("wat", sl), ("kv", b, 0), ("kv", b, 1),
                                ("kv", b, 2), ("kv", b, 3)] if d in (0, ND - 1) else (),
                            war=[pa6_key(bank)] if d == 0 else (),
                            wr=[pa6_key(bank)] if d == ND - 1 else (),
                            deps=[t_w, t_vT[(b, 0)], t_vT[(b, 1)]] if d == 0 else (),
                            inc=(d == ND - 1))
                    t_x1[(b, e4)] = emit("ve", lambda e, _b=b, _e=e4, _bk=bank, _i=i: e.tensor_tensor(
                        xsl(_i, _b, _e), xsl(_i, _b, _e), pa6_ap(_bk), ALU.add),
                        rd=[pa6_key(bank)], wr=[("x", b, e4)], deps=[tm])

            t_xqT = {}
            for b in range(NBA):
                pb = b % 2
                for c in range(4):
                    emit("ve", lambda e, _b=b, _c=c, _i=i: e.bn_stats(
                        stA[:, _b * 24 + _c * 6:_b * 24 + (_c + 1) * 6],
                        xsl(_i, _b, _c)),
                        rd=[("x", b, c)], wr=[("stA", b, c)],
                        deps=[t_x1[(b, c)]])
                t_mv = emit("ve", lambda e, _b=b: e.bn_aggr(
                    mv2[:, 2 * _b:2 * _b + 2], stA[:, _b * 24:(_b + 1) * 24]),
                    rd=[("stA", b, cc) for cc in range(4)], wr=[("mv2", b)])
                t_sq = emit("act", lambda e, _b=b: e.activation(
                    r2[:, _b:_b + 1], mv2[:, 2 * _b + 1:2 * _b + 2], AF.Sqrt,
                    bias=np.float32(1e-11), scale=np.float32(1.000001)),
                    rd=[("mv2", b)], wr=[("r2", b)], deps=[t_mv])
                t_r = emit("ve", lambda e, _b=b: e.reciprocal(
                    r2[:, _b:_b + 1], r2[:, _b:_b + 1]), wr=[("r2", b)], deps=[t_sq])
                t_rg = emit("ve", lambda e, _b=b: e.tensor_single_scalar(
                    r2[:, _b:_b + 1], r2[:, _b:_b + 1], thgf[:, 3:4], ALU.mult),
                    rd=[("fcthg",)], wr=[("r2", b)], deps=[t_r])
                t_nm = emit("ve", lambda e, _b=b: e.tensor_scalar(
                    nm2[:, _b:_b + 1], mv2[:, 2 * _b:2 * _b + 1], r2[:, _b:_b + 1],
                    -1.0, ALU.mult, ALU.mult),
                    rd=[("mv2", b), ("r2", b)], wr=[("nm2", b)], deps=[t_rg])
                t_a2 = emit("act", lambda e, _b=b, _pb=pb, _i=i: e.activation(
                    xn[:, _pb * D:(_pb + 1) * D], xsl(_i, _b), AF.Identity,
                    bias=nm2[:, _b:_b + 1], scale=r2[:, _b:_b + 1]),
                    rd=[("r2", b), ("nm2", b)] + [("x", b, ee) for ee in range(NED)],
                    wr=[("xa", pb)],
                    deps=[t_nm] + [t_x1[(b, ee)] for ee in range(NED)])
                t_xqT.update({(b, hh): tk for hh, tk in transpose_block(
                    lambda _d, _pb=pb: xn[:, _pb * D + _d * 128:_pb * D + (_d + 1) * 128],
                    lambda _h, _b=b: xqT[:, _b * D + _h * 1024:_b * D + (_h + 1) * 1024],
                    [("xa", pb)], lambda _h, _b=b: [("xT", _b, _h)],
                    [t_a2]).items()})
            return t_x1, t_xqT

        def phase_E(i, sub, t_xqT, t_p0a, t_p0b):
            t_h = {}
            for e16 in range(NEF):
                sl = next_wat()
                t_w = dma("pool", s_wfc[sl], wat[:, sl * ND * 512:(sl + 1) * ND * 512],
                          wfc_b4[e16], wr=[("wat", sl)], deps=[t_p0a, t_p0b])
                for lb in range(NBF):
                    gb = sub * NBF + lb
                    bank = next_pa()
                    tm = None
                    for d in range(ND):
                        tm = emit("pe", lambda e, _sl=sl, _gb=gb, _d=d, _bk=bank: e.matmul(
                            pa6_ap(_bk),
                            xqT[:, _gb * D + _d * 128:_gb * D + (_d + 1) * 128],
                            wat_view(_sl)[:, _d, :],
                            start=(_d == 0), stop=(_d == ND - 1)),
                            rd=[("wat", sl), ("xT", gb, 0), ("xT", gb, 1)] if d in (0, ND - 1) else (),
                            war=[pa6_key(bank)] if d == 0 else (),
                            wr=[pa6_key(bank)] if d == ND - 1 else (),
                            deps=[t_w, t_xqT[(gb, 0)], t_xqT[(gb, 1)]] if d == 0 else (),
                            inc=(d == ND - 1))
                    t_h[(lb, e16)] = emit("act", lambda e, _lb=lb, _e=e16, _bk=bank: e.activation(
                        h[:, _lb * FF + _e * 512:_lb * FF + (_e + 1) * 512],
                        pa6_ap(_bk), AF.Gelu),
                        rd=[pa6_key(bank)], wr=[("h", lb, e16)], deps=[tm])

            t_a3 = {}
            for lb in range(NBF):
                for c in range(16):
                    emit("ve", lambda e, _lb=lb, _c=c: e.bn_stats(
                        stH[:, _lb * 96 + _c * 6:_lb * 96 + (_c + 1) * 6],
                        h[:, _lb * FF + _c * 512:_lb * FF + (_c + 1) * 512]),
                        rd=[("h", lb, c)], wr=[("stH", lb, c)], deps=[t_h[(lb, c)]])
                t_mv = emit("ve", lambda e, _lb=lb: e.bn_aggr(
                    mv3[:, 2 * _lb:2 * _lb + 2], stH[:, _lb * 96:(_lb + 1) * 96]),
                    rd=[("stH", lb, cc) for cc in range(16)], wr=[("mv3", lb)])
                t_sq = emit("act", lambda e, _lb=lb: e.activation(
                    r3[:, _lb:_lb + 1], mv3[:, 2 * _lb + 1:2 * _lb + 2], AF.Sqrt,
                    bias=np.float32(1e-6)),
                    rd=[("mv3", lb)], wr=[("r3", lb)], deps=[t_mv])
                t_r = emit("ve", lambda e, _lb=lb: e.reciprocal(
                    r3[:, _lb:_lb + 1], r3[:, _lb:_lb + 1]), wr=[("r3", lb)], deps=[t_sq])
                t_rg = emit("ve", lambda e, _lb=lb: e.tensor_single_scalar(
                    r3[:, _lb:_lb + 1], r3[:, _lb:_lb + 1], thgp[:, 3:4], ALU.mult),
                    rd=[("pjthg",)], wr=[("r3", lb)], deps=[t_r])
                t_nm = emit("ve", lambda e, _lb=lb: e.tensor_scalar(
                    nm3[:, _lb:_lb + 1], mv3[:, 2 * _lb:2 * _lb + 1], r3[:, _lb:_lb + 1],
                    -1.0, ALU.mult, ALU.mult),
                    rd=[("mv3", lb), ("r3", lb)], wr=[("nm3", lb)], deps=[t_rg])
                qs = []
                for qq in range(4):
                    qs.append(emit("ve", lambda e, _lb=lb, _q=qq: e.tensor_scalar(
                        h[:, _lb * FF + _q * 2048:_lb * FF + (_q + 1) * 2048],
                        h[:, _lb * FF + _q * 2048:_lb * FF + (_q + 1) * 2048],
                        r3[:, _lb:_lb + 1], nm3[:, _lb:_lb + 1], ALU.mult, ALU.add),
                        rd=[("r3", lb), ("nm3", lb)] + [("h", lb, cc) for cc in range(4 * qq, 4 * qq + 4)],
                        wr=[("h", lb, cc) for cc in range(4 * qq, 4 * qq + 4)],
                        deps=[t_nm] + [t_h[(lb, cc)] for cc in range(4 * qq, 4 * qq + 4)]))
                t_a3[lb] = qs
            return t_a3

        def phase_F(i, sub, t_a3, t_p0a, t_p0b):
            x2_toks = {lb: [None] * NED for lb in range(NBF)}
            for pas in range(2):
                stop_toks = {}
                for g in range(8):
                    gp = g % 2
                    t_hqT = {}
                    for lb in range(NBF):
                        tt = None
                        for d8 in range(8):
                            tt = emit("pe", lambda e, _lb=lb, _g=g, _d8=d8: e.transpose(
                                pt[_lb][:, _d8 * 128:(_d8 + 1) * 128],
                                h[:, _lb * FF + _g * 1024 + _d8 * 128:
                                  _lb * FF + _g * 1024 + (_d8 + 1) * 128], c_id[:]),
                                rd=[("h", lb, (g * 1024 + d8 * 128) // 512)] if d8 in (0, 7) else (),
                                war=[("pt", lb)] if d8 == 0 else (),
                                wr=[("pt", lb)] if d8 == 7 else (),
                                deps=[t_a3[lb][g // 2]] if d8 == 0 else (),
                                inc=(d8 == 7))
                        t_hqT[lb] = emit("ve", lambda e, _lb=lb, _gp=gp: e.tensor_copy(
                            hqT[:, (_lb * 2 + _gp) * 1024:(_lb * 2 + _gp + 1) * 1024],
                            pt[_lb][:, :]),
                            rd=[("pt", lb)], wr=[("hqT", lb, gp)], deps=[tt])
                    for half4 in range(2):
                        g4 = g * 2 + half4
                        wsl = g4 % 2
                        t_w = dma("pool", s_wpj[wsl], wpj[:, wsl * 4096:(wsl + 1) * 4096],
                                  wpj_b4[pas, g4], wr=[("wpj", wsl)],
                                  deps=[t_p0a, t_p0b])
                        for dd4 in range(4):
                            dd = half4 * 4 + dd4
                            dprime = g * 8 + dd
                            for lb in range(NBF):
                                for ep in range(2):
                                    bank = lb * 2 + ep
                                    last = (ep == 1)
                                    tok = emit("pe", lambda e, _lb=lb, _gp=gp, _dd=dd, _w=wsl, _d4=dd4, _ep=ep, _dp=dprime, _bk=bank: e.matmul(
                                        pa[_bk][:, :],
                                        hqT[:, (_lb * 2 + _gp) * 1024 + _dd * 128:
                                            (_lb * 2 + _gp) * 1024 + (_dd + 1) * 128],
                                        wpj[:, _w * 4096 + _d4 * 1024 + _ep * 512:
                                            _w * 4096 + _d4 * 1024 + (_ep + 1) * 512],
                                        start=(_dp == 0), stop=(_dp == NF - 1)),
                                        rd=[("hqT", lb, gp), ("wpj", wsl)] if last else (),
                                        war=[("pa", bank)] if dprime == 0 else (),
                                        wr=[("pa", bank)] if dprime == NF - 1 else (),
                                        deps=[t_hqT[lb], t_w] if ep == 0 else (),
                                        inc=(last or dprime == NF - 1))
                                    if dprime == NF - 1:
                                        stop_toks[(lb, ep)] = tok
                for lb in range(NBF):
                    gb = sub * NBF + lb
                    for ep in range(2):
                        e4 = pas * 2 + ep
                        bank = lb * 2 + ep
                        x2_toks[lb][e4] = emit("ve", lambda e, _gb=gb, _e=e4, _bk=bank, _i=i: e.tensor_tensor(
                            xsl(_i, _gb, _e), xsl(_i, _gb, _e), pa[_bk][:, :], ALU.add),
                            rd=[("pa", bank)], wr=[("x", gb, e4)],
                            deps=[stop_toks[(lb, ep)]])
            for lb in range(NBF):
                gb = sub * NBF + lb
                dma("pool", s_out[gb % 2], out[(i * NBA + gb) * 128:(i * NBA + gb + 1) * 128, :],
                    xsl(i, gb), rd=[("x", gb, ee) for ee in range(NED)],
                    wr=[("xf", gb)], deps=x2_toks[lb])

        def dbg_post_B(i, t_q, t_kv):
            if not dbg:
                return
            for b in range(NBA):
                rows = slice((i * NBA + b) * 128, (i * NBA + b + 1) * 128)
                dma("sp", s_dbg, dbg_q[rows, :], qv[:, b * D:(b + 1) * D],
                    rd=[("qv", b, ee) for ee in range(NED)],
                    deps=[t_q[(b, ee)] for ee in range(NED)])
                dma("sp", s_dbg, dbg_kv[rows, :], kv[:, b * D:(b + 1) * D],
                    rd=[("kv", b, ee) for ee in range(NED)],
                    deps=[t_kv[(b, ee)] for ee in range(NED)])

        def dbg_post_C(i, t_vhat):
            if not dbg:
                return
            for b in range(NBA):
                rows = slice((i * NBA + b) * 128, (i * NBA + b + 1) * 128)
                dma("sp", s_dbg, dbg_vh[rows, :], qv[:, b * D:(b + 1) * D],
                    rd=[("qv", b, ee) for ee in range(NED)],
                    deps=[t_vhat[(b, ee)] for ee in range(NED)])

        def dbg_post_x1(i, t_x1):
            if not dbg:
                return
            for b in range(NBA):
                rows = slice((i * NBA + b) * 128, (i * NBA + b + 1) * 128)
                dma("sp", s_dbg, dbg_x1[rows, :], xsl(i, b),
                    rd=[("x", b, ee) for ee in range(NED)],
                    deps=[t_x1[(b, ee)] for ee in range(NED)])

        def dbg_post_E(i, sub, t_a3):
            if not dbg:
                return
            for lb in range(NBF):
                gb = sub * NBF + lb
                rows = slice((i * NBA + gb) * 128, (i * NBA + gb + 1) * 128)
                dma("sp", s_dbg, dbg_h[rows, :], h[:, lb * FF:(lb + 1) * FF],
                    rd=[("h", lb, cc) for cc in range(16)], deps=list(t_a3[lb]))

        p0toks = [None, None]

        def set_p0toks():
            p0toks[0] = (s_p0o[0], s_p0o[0].count)
            p0toks[1] = (s_p0o[1], s_p0o[1].count)

        t_xin = tile_load(0)
        t_xnT = phase_A(0, t_xin)
        t_q, t_kv = phase_B(0, t_xnT)
        dbg_post_B(0, t_q, t_kv)
        t_vhat = phase_C(0, t_q, t_kv)
        dbg_post_C(0, t_vhat)
        phase0_matrix(wfct, D, FF, thgf, "fc", fc_outdst)
        fctoks = [(s_p0o[0], s_p0o[0].count), (s_p0o[1], s_p0o[1].count)]
        t_x1, t_xqT = phase_D(0, t_vhat)
        dbg_post_x1(0, t_x1)
        phase0_matrix(wpjt, FF, D, thgp, "pj", pj_outdst)
        set_p0toks()
        for sub in range(NSUB):
            t_a3 = phase_E(0, sub, t_xqT, fctoks[0], fctoks[1])
            dbg_post_E(0, sub, t_a3)
            phase_F(0, sub, t_a3, p0toks[0], p0toks[1])

        for i in range(1, NT):
            t_xin = tile_load(i)
            t_xnT = phase_A(i, t_xin)
            t_q, t_kv = phase_B(i, t_xnT)
            dbg_post_B(i, t_q, t_kv)
            t_vhat = phase_C(i, t_q, t_kv)
            dbg_post_C(i, t_vhat)
            t_x1, t_xqT = phase_D(i, t_vhat)
            dbg_post_x1(i, t_x1)
            for sub in range(NSUB):
                t_a3 = phase_E(i, sub, t_xqT, p0toks[0], p0toks[1])
                dbg_post_E(i, sub, t_a3)
                phase_F(i, sub, t_a3, p0toks[0], p0toks[1])

        for so in s_out:
            if so.count:
                bd.raw_wait("pool", (so, so.count))

        with nc.Block() as block:
            block.tensor(lambda eng: [op(eng) for op in bd.st["pe"].ops])
            block.vector(lambda eng: [op(eng) for op in bd.st["ve"].ops])
            block.scalar(lambda eng: [op(eng) for op in bd.st["act"].ops])
            block.gpsimd(lambda eng: [op(eng) for op in bd.st["pool"].ops])
            block.sync(lambda eng: [op(eng) for op in bd.st["sp"].ops])

    return nc


_CACHE = {}


def _get_program(S, T):
    key = (S, T)
    if key not in _CACHE:
        _CACHE[key] = build_program(S, T)
    return _CACHE[key]


def kernel(L0_attn_x, L0_attn_Wq, L0_attn_Wk, L0_attn_Wv, L0_attn_Wo,
           L0_ffn_w_fc, L0_ffn_w_proj):
    x = np.ascontiguousarray(np.asarray(L0_attn_x))
    B, S, _ = x.shape
    ins = {
        "wqt": np.ascontiguousarray(np.asarray(L0_attn_Wq).T),
        "wkt": np.ascontiguousarray(np.asarray(L0_attn_Wk).T),
        "wvt": np.ascontiguousarray(np.asarray(L0_attn_Wv).T),
        "wot": np.ascontiguousarray(np.asarray(L0_attn_Wo).T),
        "wfct": np.ascontiguousarray(np.asarray(L0_ffn_w_fc).T),
        "wpjt": np.ascontiguousarray(np.asarray(L0_ffn_w_proj).T),
    }
    nc = _get_program(S, 512)
    in_maps = [dict(ins, x=np.ascontiguousarray(x[b])) for b in range(B)]
    res = run_bass_kernel_spmd(nc, in_maps, core_ids=list(range(B)))
    return np.stack([res.results[b]["out"] for b in range(B)], axis=0)


# version 23
# speedup vs baseline: 1.0407x; 1.0407x over previous
import contextlib
import math

import ml_dtypes
import numpy as np

import concourse.bass as bass
import concourse.mybir as mybir
from concourse.bass_utils import run_bass_kernel_spmd

F32 = mybir.dt.float32
BF16 = mybir.dt.bfloat16
AF = mybir.ActivationFunctionType
ALU = mybir.AluOpType
AX = mybir.AxisListType
FP8 = mybir.dt.float8e4

D = 2048
FF = 8192
ND = D // 128
NF = FF // 128
NED = D // 512
NEF = FF // 512


class _Src:
    __slots__ = ("name", "sem", "count", "ops", "satisfied")

    def __init__(self, name):
        self.name = name
        self.sem = None
        self.count = 0
        self.ops = []
        self.satisfied = {}


class Build:
    def __init__(self, nc):
        self.nc = nc
        self.st = {n: _Src(n) for n in ("pe", "ve", "act", "pool", "sp")}
        self.slots = {}
        self.lastw = {}
        self.lastr = {}

    def slot(self, name):
        s = _Src(name)
        self.slots[name] = s
        return s

    def _deps_rd(self, keys):
        return [self.lastw[k] for k in keys if self.lastw.get(k) is not None]

    def _deps_wr(self, keys):
        deps = []
        for k in keys:
            t = self.lastw.get(k)
            if t is not None:
                deps.append(t)
            deps.extend(self.lastr.get(k, ()))
        return deps

    def _dedup(self, st, alldeps):
        waits = []
        for tok in alldeps:
            if tok is None:
                continue
            src, v = tok
            key = id(src)
            if st.satisfied.get(key, -1) >= v:
                continue
            st.satisfied[key] = v
            waits.append((src, v))
        return waits

    def emit(self, stn, fn, rd=(), wr=(), war=(), deps=(), inc=True):
        st = self.st[stn]
        alldeps = list(deps) + self._deps_rd(rd) + self._deps_wr(wr) + self._deps_wr(war)
        waits = self._dedup(st, alldeps)
        if inc:
            st.count += 1
        myc = st.count
        mysem = st

        def op(eng, _waits=tuple(waits), _fn=fn, _inc=inc):
            for src, v in _waits:
                eng.wait_ge(src.sem, v)
            ins = _fn(eng)
            if _inc:
                ins.then_inc(mysem.sem, 1)

        st.ops.append(op)
        if inc:
            tok = (st, myc)
            for k in rd:
                self.lastr.setdefault(k, []).append(tok)
            for k in wr:
                self.lastw[k] = tok
                self.lastr[k] = []
            return tok
        assert not wr
        return None

    def dma(self, stn, slot, dst, src, rd=(), wr=(), deps=(), **kw):
        st = self.st[stn]
        alldeps = list(deps) + self._deps_rd(rd) + self._deps_wr(wr)
        if slot.count:
            alldeps.append((slot, slot.count))
        waits = self._dedup(st, alldeps)
        slot.count += 16
        myc = slot.count

        def op(eng, _waits=tuple(waits), _dst=dst, _src=src, _kw=kw):
            for s, v in _waits:
                eng.wait_ge(s.sem, v)
            eng.dma_start(_dst, _src, **_kw).then_inc(slot.sem, 16)

        st.ops.append(op)
        tok = (slot, myc)
        for k in rd:
            self.lastr.setdefault(k, []).append(tok)
        for k in wr:
            self.lastw[k] = tok
            self.lastr[k] = []
        return tok

    def raw_wait(self, stn, tok):
        src, v = tok
        self.st[stn].ops.append(lambda eng, _s=src, _v=v: eng.wait_ge(_s.sem, _v))


def build_program(S, T, dbg=False):
    NBA = T // 128
    NBF = 2
    NSUB = NBA // NBF
    NT = S // T
    assert S % T == 0 and T % 128 == 0 and NBA == 4
    DEN = float(np.float32(math.sqrt(float(D)) / float(S)))

    nc = bass.Bass("TRN2", target_bir_lowering=False)

    x_in = nc.dram_tensor("x", [S, D], F32, kind="ExternalInput").ap()
    wqt = nc.dram_tensor("wqt", [D, D], F32, kind="ExternalInput").ap()
    wkt = nc.dram_tensor("wkt", [D, D], F32, kind="ExternalInput").ap()
    wvt = nc.dram_tensor("wvt", [D, D], F32, kind="ExternalInput").ap()
    wot = nc.dram_tensor("wot", [D, D], F32, kind="ExternalInput").ap()
    wfct = nc.dram_tensor("wfct", [D, FF], F32, kind="ExternalInput").ap()
    wpjt = nc.dram_tensor("wpjt", [FF, D], F32, kind="ExternalInput").ap()
    out = nc.dram_tensor("out", [S, D], F32, kind="ExternalOutput").ap()
    if dbg:
        dbg_q = nc.dram_tensor("dbg_q", [S, D], BF16, kind="ExternalOutput").ap()
        dbg_kv = nc.dram_tensor("dbg_kv", [S, D], BF16, kind="ExternalOutput").ap()
        dbg_vh = nc.dram_tensor("dbg_vh", [S, D], BF16, kind="ExternalOutput").ap()
        dbg_x1 = nc.dram_tensor("dbg_x1", [S, D], F32, kind="ExternalOutput").ap()
        dbg_h = nc.dram_tensor("dbg_h", [S, FF], BF16, kind="ExternalOutput").ap()

    wq_b4 = nc.dram_tensor("wq_b4", [NED, 128, ND * 512], BF16).ap()
    wk_b4 = nc.dram_tensor("wk_b4", [NED, 128, ND * 512], BF16).ap()
    wv_b4 = nc.dram_tensor("wv_b4", [NED, 128, ND * 512], BF16).ap()
    wo_b4 = nc.dram_tensor("wo_b4", [NED, 128, ND * 512], BF16).ap()
    wfc_b4 = nc.dram_tensor("wfc_b4", [NEF, 128, ND * 512], FP8).ap()
    wpj_b4 = nc.dram_tensor("wpj_b4", [2, NF // 4, 128, 4 * 1024], FP8).ap()

    d_id = nc.inline_tensor(np.eye(128, dtype=ml_dtypes.bfloat16), name="c_id").ap()
    d_u = nc.inline_tensor(
        np.triu(np.ones((128, 128), np.float32)).astype(ml_dtypes.bfloat16),
        name="c_u").ap()
    d_a1 = nc.inline_tensor(np.ones((128, 128), dtype=ml_dtypes.bfloat16),
                            name="c_a1").ap()
    d_i32 = nc.inline_tensor(np.eye(128, dtype=np.float32), name="c_i32").ap()
    d_oc = nc.inline_tensor(np.ones((128, 1), dtype=np.float32), name="c_oc").ap()
    d_or = nc.inline_tensor(np.ones((1, 128), dtype=np.float32), name="c_or").ap()

    with contextlib.ExitStack() as es:
        E = es.enter_context
        xb = E(nc.sbuf_tensor("xb", [128, NBA * D], F32))
        xn = E(nc.sbuf_tensor("xn", [128, 2 * D], BF16))
        xnT = E(nc.sbuf_tensor("xnT", [128, NBA * D], BF16))
        qv = E(nc.sbuf_tensor("qv", [128, NBA * D], BF16))
        kv = E(nc.sbuf_tensor("kv", [128, NBA * D], BF16))
        h = E(nc.sbuf_tensor("h", [128, NBF * FF], BF16))
        hqT = E(nc.sbuf_tensor("hqT", [128, NBF * 2 * 1024], BF16))
        carry = E(nc.sbuf_tensor("carry", [128, D], F32))
        wat = E(nc.sbuf_tensor("wat", [128, 3 * ND * 512], BF16))
        wpj = E(nc.sbuf_tensor("wpj", [128, 2 * 4096], BF16))
        c_id = E(nc.sbuf_tensor("cid", [128, 128], BF16))
        c_u = E(nc.sbuf_tensor("cu", [128, 128], BF16))
        c_a1 = E(nc.sbuf_tensor("ca1", [128, 128], BF16))
        c_i32 = E(nc.sbuf_tensor("ci32", [128, 128], F32))
        c_oc = E(nc.sbuf_tensor("coc", [128, 1], F32))
        c_or = E(nc.sbuf_tensor("cor", [1, 128], F32))
        stA = E(nc.sbuf_tensor("stA", [128, NBA * 24], F32))
        stH = E(nc.sbuf_tensor("stH", [128, NBF * 96], F32))
        mv1 = E(nc.sbuf_tensor("mv1", [128, NBA * 2], F32))
        mv2 = E(nc.sbuf_tensor("mv2", [128, NBA * 2], F32))
        mv3 = E(nc.sbuf_tensor("mv3", [128, NBF * 2], F32))
        r1 = E(nc.sbuf_tensor("r1", [128, NBA], F32))
        r2 = E(nc.sbuf_tensor("r2", [128, NBA], F32))
        r3 = E(nc.sbuf_tensor("r3", [128, NBF], F32))
        nm1 = E(nc.sbuf_tensor("nm1", [128, NBA], F32))
        nm2 = E(nc.sbuf_tensor("nm2", [128, NBA], F32))
        nm3 = E(nc.sbuf_tensor("nm3", [128, NBF], F32))
        thgf = E(nc.sbuf_tensor("thgf", [128, 4], F32))
        thgp = E(nc.sbuf_tensor("thgp", [128, 4], F32))
        gparts = E(nc.sbuf_tensor("gparts", [128, 64], F32))
        gsum = E(nc.sbuf_tensor("gsum", [128, 1], F32))
        gtmp = E(nc.sbuf_tensor("gtmp", [1, 4], F32))

        pt = [E(nc.psum_tensor(f"pt{j}", [128, 1024], BF16)) for j in range(2)]
        pa = [E(nc.psum_tensor(f"pa{j}", [128, 512], F32)) for j in range(4)]
        pc = [E(nc.psum_tensor(f"pc{j}", [128, 512], F32)) for j in range(2)]

        bd = Build(nc)
        for n, st in bd.st.items():
            st.sem = E(nc.semaphore(f"se_{n}"))
        s_const = bd.slot("const")
        s_x = [bd.slot(f"x{j}") for j in range(2)]
        s_wat = [bd.slot(f"wat{j}") for j in range(3)]
        s_wfc = [bd.slot(f"wfc{j}") for j in range(3)]
        s_wpj = [bd.slot(f"wpj{j}") for j in range(2)]
        s_out = [bd.slot(f"out{j}") for j in range(2)]
        s_cast = bd.slot("cast")
        s_p0i = [bd.slot(f"p0i{j}") for j in range(2)]
        s_p0o = [bd.slot(f"p0o{j}") for j in range(2)]
        s_dbg = bd.slot("dbg")
        for s in bd.slots.values():
            s.sem = E(nc.semaphore(f"sd_{s.name}"))

        emit, dma = bd.emit, bd.dma

        t_c = None
        for dst, src in ((c_id, d_id), (c_u, d_u), (c_a1, d_a1),
                         (c_i32, d_i32), (c_oc, d_oc), (c_or, d_or)):
            t_c = dma("sp", s_const, dst[:], src)
        emit("ve", lambda e: e.memzero(carry[:]),
             wr=[("carry", ee) for ee in range(NED)])

        cast_toks = {}
        prev = None
        for nm, dst4, srct in (("k", wk_b4, wkt), ("v", wv_b4, wvt),
                               ("q", wq_b4, wqt), ("o", wo_b4, wot)):
            src4 = srct.rearrange("(d p) (e j) -> e p d j", p=128, j=512)
            for e4 in range(NED):
                prev = dma("pool", s_cast,
                           dst4[e4].rearrange("p (d j) -> p d j", j=512),
                           src4[e4], deps=[prev] if prev else [])
            cast_toks[nm] = prev

        def p0in(sl):
            return h[:, FF + sl * 4096: FF + (sl + 1) * 4096].bitcast(F32)

        def p0in_hkeys(sl):
            return [("h", 1, cc) for cc in range(8 * sl, 8 * sl + 8)]

        def p0out(sl):
            return h[:, sl * 2048:(sl + 1) * 2048]

        def p0out_hkeys(sl):
            return [("h", 0, cc) for cc in range(4 * sl, 4 * sl + 4)]

        def p0eng(j):
            return "ve"

        def phase0_matrix(w_in, nrows, ncols, thg, tag, outdst):
            nchunk_c = ncols // 2048
            nchunks = (nrows // 128) * nchunk_c
            assert nchunks <= 64
            wr_src = w_in.rearrange("(r p) c -> r p c", p=128)
            for j in range(nchunks):
                r, c = divmod(j, nchunk_c)
                sl = j % 2
                ti = dma("sp", s_p0i[sl], p0in(sl),
                         wr_src[r, :, c * 2048:(c + 1) * 2048],
                         wr=[(tag + "p0i", sl)] + p0in_hkeys(sl))
                emit("act", lambda e, _s=sl, _j=j: e.activation(
                    p0out(3), p0in(_s), AF.Abs,
                    accum_out=gparts[:, _j:_j + 1]),
                    rd=[(tag + "p0i", sl)] + p0in_hkeys(sl),
                    wr=[("gparts", j), ("p0o", 3)] + p0out_hkeys(3), deps=[ti])
            t_gs = emit("ve", lambda e, _n=nchunks: e.tensor_reduce(
                gsum[:], gparts[:, 0:_n], AX.X, ALU.add),
                rd=[("gparts", jj) for jj in range(nchunks)], wr=[("gsum",)])
            t_pm = emit("pe", lambda e: e.matmul(
                pa[0][0:1, 0:1], c_oc[:], gsum[:], start=True, stop=True),
                rd=[("gsum",)], wr=[("pa", 0)], deps=[t_c, t_gs])
            nel = float(1.0 / (nrows * ncols))
            t1 = emit("ve", lambda e, _ne=nel: e.tensor_single_scalar(
                gtmp[0:1, 2:3], pa[0][0:1, 0:1], _ne, ALU.mult),
                rd=[("pa", 0)], wr=[("gtmp2",)], deps=[t_pm])
            t2 = emit("ve", lambda e: e.tensor_scalar(
                gtmp[0:1, 0:1], gtmp[0:1, 2:3], 1e-6, 0.5,
                ALU.add, ALU.mult), rd=[("gtmp2",)], wr=[("gtmp0",)], deps=[t1])
            t3 = emit("ve", lambda e: e.tensor_single_scalar(
                gtmp[0:1, 1:2], gtmp[0:1, 0:1], -1.0, ALU.mult),
                rd=[("gtmp0",)], wr=[("gtmp1",)], deps=[t2])
            t4 = emit("ve", lambda e: e.tensor_single_scalar(
                gtmp[0:1, 3:4], gtmp[0:1, 2:3], 0.5, ALU.mult),
                rd=[("gtmp2",)], wr=[("gtmp3",)], deps=[t1])
            t_bc = emit("pe", lambda e: e.matmul(
                pa[1][0:128, 0:4], c_or[:], gtmp[0:1, 0:4], start=True, stop=True),
                rd=[("gtmp0",), ("gtmp1",), ("gtmp2",), ("gtmp3",)], wr=[("pa", 1)],
                deps=[t1, t2, t3, t4, t_c])
            t_thg = emit("ve", lambda e, _t=thg: e.tensor_copy(_t[:, 0:4], pa[1][:, 0:4]),
                         rd=[("pa", 1)], wr=[(tag + "thg",)], deps=[t_bc])
            for j in range(nchunks):
                r, c = divmod(j, nchunk_c)
                sl = j % 2
                oa, ob = 2 * sl, 2 * sl + 1
                eng = p0eng(j)
                ti = dma("sp", s_p0i[sl], p0in(sl),
                         wr_src[r, :, c * 2048:(c + 1) * 2048],
                         wr=[(tag + "p0i", sl)] + p0in_hkeys(sl))
                tA = emit("act", lambda e, _s=sl, _o=oa, _t=thg: e.activation(
                    p0out(_o), p0in(_s), AF.Sign, bias=_t[:, 1:2]),
                    rd=[(tag + "p0i", sl), (tag + "thg",)] + p0in_hkeys(sl),
                    wr=[("p0o", oa)] + p0out_hkeys(oa), deps=[ti, t_thg])
                tB = emit("act", lambda e, _s=sl, _o=ob, _t=thg: e.activation(
                    p0out(_o), p0in(_s), AF.Sign, bias=_t[:, 0:1]),
                    rd=[(tag + "p0i", sl)] + p0in_hkeys(sl),
                    wr=[("p0o", ob)] + p0out_hkeys(ob), deps=[ti])
                tC = emit("ve", lambda e, _o=oa: e.tensor_tensor(
                    p0out(_o), p0out(_o), p0out(_o + 1), ALU.add),
                    rd=[("p0o", ob)] + p0out_hkeys(ob),
                    wr=[("p0o", oa)] + p0out_hkeys(oa), deps=[tA, tB])
                dst, src_re = outdst(r, c, p0out(oa))
                dma("pool", s_p0o[sl], dst, src_re,
                    rd=[("p0o", oa)] + p0out_hkeys(oa), deps=[tC])

        def fc_outdst(r, c, srcap):
            dst = wfc_b4[4 * c:4 * c + 4, :, r * 512:(r + 1) * 512]
            return (dst.rearrange("e p j -> p e j"),
                    srcap.rearrange("p (e j) -> p e j", j=512))

        def pj_outdst(r, c, srcap):
            g4, dd = divmod(r, 4)
            dst = wpj_b4[0:2, g4, :, dd * 1024:(dd + 1) * 1024]
            return (dst.rearrange("s p j -> p s j"),
                    srcap.rearrange("p (s j) -> p s j", j=1024))

        def xsl(i, b, e4=None):
            base = b * D
            if e4 is None:
                return xb[:, base:base + D]
            return xb[:, base + e4 * 512: base + (e4 + 1) * 512]

        xqT = xnT

        def wat_view(sl):
            return wat[:, sl * ND * 512:(sl + 1) * ND * 512].rearrange(
                "p (n w) -> p n w", n=ND)



        def transpose_block(src_fn, dst_fn, src_keys, dst_key_fn, extra_deps):
            toks = {}
            for half in range(2):
                tt = None
                for d8 in range(8):
                    d = half * 8 + d8
                    tt = emit("pe", lambda e, _h=half, _d8=d8, _d=d: e.transpose(
                        pt[_h][:, _d8 * 128:(_d8 + 1) * 128], src_fn(_d), c_id[:]),
                        rd=src_keys if d8 in (0, 7) else (),
                        war=[("pt", half)] if d8 == 0 else (),
                        wr=[("pt", half)] if d8 == 7 else (),
                        deps=(list(extra_deps) + [t_c]) if d8 == 0 else (),
                        inc=(d8 == 7))
                toks[half] = emit("ve", lambda e, _h=half: e.tensor_copy(
                    dst_fn(_h), pt[_h][:, :]),
                    rd=[("pt", half)], wr=dst_key_fn(half), deps=[tt])
            return toks

        wat_ring = [0]
        pa_ring = [0]

        def next_wat():
            sl = wat_ring[0] % 3
            wat_ring[0] += 1
            return sl

        def next_pa():
            bk = pa_ring[0] % 4
            pa_ring[0] += 1
            return bk

        def tile_load(i):
            toks = []
            for b in range(NBA):
                toks.append(dma("sp", s_x[b % 2],
                                xb[:, b * D:(b + 1) * D],
                                x_in[(i * NBA + b) * 128:(i * NBA + b + 1) * 128, :],
                                wr=[("x", b, ee) for ee in range(NED)] + [("xf", b)]))
            return toks

        def phase_A(i, t_xin):
            t_xnT = {}
            for b in range(NBA):
                pb = b % 2
                for c in range(4):
                    emit("ve", lambda e, _b=b, _c=c, _i=i: e.bn_stats(
                        stA[:, _b * 24 + _c * 6:_b * 24 + (_c + 1) * 6],
                        xsl(_i, _b, _c)),
                        rd=[("x", b, c)], wr=[("stA", b, c)], deps=[t_xin[b]])
                t_mv = emit("ve", lambda e, _b=b: e.bn_aggr(
                    mv1[:, 2 * _b:2 * _b + 2], stA[:, _b * 24:(_b + 1) * 24]),
                    rd=[("stA", b, cc) for cc in range(4)], wr=[("mv1", b)])
                t_sq = emit("act", lambda e, _b=b: e.activation(
                    r1[:, _b:_b + 1], mv1[:, 2 * _b + 1:2 * _b + 2], AF.Sqrt,
                    bias=np.float32(1e-5)),
                    rd=[("mv1", b)], wr=[("r1", b)], deps=[t_mv])
                t_r = emit("ve", lambda e, _b=b: e.reciprocal(
                    r1[:, _b:_b + 1], r1[:, _b:_b + 1]), wr=[("r1", b)], deps=[t_sq])
                t_nm = emit("ve", lambda e, _b=b: e.tensor_scalar(
                    nm1[:, _b:_b + 1], mv1[:, 2 * _b:2 * _b + 1], r1[:, _b:_b + 1],
                    -1.0, ALU.mult, ALU.mult),
                    rd=[("mv1", b), ("r1", b)], wr=[("nm1", b)], deps=[t_r])
                t_xn = emit("act", lambda e, _b=b, _pb=pb, _i=i: e.activation(
                    xn[:, _pb * D:(_pb + 1) * D], xsl(_i, _b), AF.Identity,
                    bias=nm1[:, _b:_b + 1], scale=r1[:, _b:_b + 1]),
                    rd=[("r1", b), ("nm1", b)] + [("x", b, ee) for ee in range(NED)],
                    wr=[("xa", pb)], deps=[t_nm, t_xin[b]])
                t_xnT.update({(b, hh): tk for hh, tk in transpose_block(
                    lambda _d, _pb=pb: xn[:, _pb * D + _d * 128:_pb * D + (_d + 1) * 128],
                    lambda _h, _b=b: xnT[:, _b * D + _h * 1024:_b * D + (_h + 1) * 1024],
                    [("xa", pb)], lambda _h, _b=b: [("xT", _b, _h)],
                    [t_xn]).items()})
            return t_xnT

        def phase_B(i, t_xnT):
            t_q, t_kv = {}, {}
            for w4, cname in ((wk_b4, "k"), (wv_b4, "v"), (wq_b4, "q")):
                for e4 in range(NED):
                    sl = next_wat()
                    t_w = dma("sp", s_wat[sl], wat[:, sl * ND * 512:(sl + 1) * ND * 512],
                              w4[e4], wr=[("wat", sl)], deps=[cast_toks[cname]])
                    for b in range(NBA):
                        bank = next_pa()
                        tm = None
                        for d in range(ND):
                            tm = emit("pe", lambda e, _sl=sl, _b=b, _d=d, _bk=bank: e.matmul(
                                pa[_bk][:, :],
                                xnT[:, _b * D + _d * 128:_b * D + (_d + 1) * 128],
                                wat_view(_sl)[:, _d, :],
                                start=(_d == 0), stop=(_d == ND - 1)),
                                rd=[("wat", sl), ("xT", b, 0), ("xT", b, 1)] if d in (0, ND - 1) else (),
                                war=[("pa", bank)] if d == 0 else (),
                                wr=[("pa", bank)] if d == ND - 1 else (),
                                deps=[t_w, t_xnT[(b, 0)], t_xnT[(b, 1)]] if d == 0 else (),
                                inc=(d == ND - 1))
                        if cname == "k":
                            emit("ve", lambda e, _b=b, _e=e4, _bk=bank: e.tensor_copy(
                                qv[:, _b * D + _e * 512:_b * D + (_e + 1) * 512],
                                pa[_bk][:, :]),
                                rd=[("pa", bank)], wr=[("qv", b, e4)], deps=[tm])
                        elif cname == "v":
                            t_kv[(b, e4)] = emit("ve", lambda e, _b=b, _e=e4, _bk=bank: e.tensor_tensor(
                                kv[:, _b * D + _e * 512:_b * D + (_e + 1) * 512],
                                qv[:, _b * D + _e * 512:_b * D + (_e + 1) * 512],
                                pa[_bk][:, :], ALU.mult),
                                rd=[("pa", bank), ("qv", b, e4)],
                                wr=[("kv", b, e4)], deps=[tm])
                        else:
                            t_q[(b, e4)] = emit("ve", lambda e, _b=b, _e=e4, _bk=bank: e.tensor_copy(
                                qv[:, _b * D + _e * 512:_b * D + (_e + 1) * 512],
                                pa[_bk][:, :]),
                                rd=[("pa", bank)], wr=[("qv", b, e4)], deps=[tm])
            return t_q, t_kv

        def phase_C(i, t_q, t_kv):
            t_vhat = {}
            for e4 in range(NED):
                for b in range(NBA):
                    bank = b % 2
                    emit("pe", lambda e, _b=b, _e=e4, _bk=bank: e.matmul(
                        pc[_bk][:, :], c_u[:],
                        kv[:, _b * D + _e * 512:_b * D + (_e + 1) * 512],
                        start=True, stop=False),
                        rd=[("kv", b, e4)], war=[("pc", bank)],
                        deps=[t_kv[(b, e4)]], inc=False)
                    for ib in range(b):
                        emit("pe", lambda e, _ib=ib, _e=e4, _bk=bank: e.matmul(
                            pc[_bk][:, :], c_a1[:],
                            kv[:, _ib * D + _e * 512:_ib * D + (_e + 1) * 512],
                            start=False, stop=False),
                            deps=[t_kv[(ib, e4)]], inc=False)
                    t_kvt = emit("pe", lambda e, _e=e4, _bk=bank: e.matmul(
                        pc[_bk][:, :], c_i32[:],
                        carry[:, _e * 512:(_e + 1) * 512],
                        start=False, stop=True),
                        rd=[("carry", e4)], wr=[("pc", bank)])
                    t_vhat[(b, e4)] = emit("ve", lambda e, _b=b, _e=e4, _bk=bank: e.scalar_tensor_tensor(
                        qv[:, _b * D + _e * 512:_b * D + (_e + 1) * 512],
                        qv[:, _b * D + _e * 512:_b * D + (_e + 1) * 512],
                        DEN, pc[_bk][:, :], ALU.mult, ALU.mult),
                        rd=[("pc", bank)], wr=[("qv", b, e4)],
                        deps=[t_kvt, t_q[(b, e4)]])
                emit("pe", lambda e, _e=e4: e.matmul(
                    pa[3][:, :], c_i32[:], carry[:, _e * 512:(_e + 1) * 512],
                    start=True, stop=False),
                    rd=[("carry", e4)], war=[("pa", 3)], inc=False)
                tcu = None
                for b in range(NBA):
                    tcu = emit("pe", lambda e, _b=b, _e=e4: e.matmul(
                        pa[3][:, :], c_a1[:],
                        kv[:, _b * D + _e * 512:_b * D + (_e + 1) * 512],
                        start=False, stop=(_b == NBA - 1)),
                        rd=[("kv", b, e4)] if b == NBA - 1 else (),
                        wr=[("pa", 3)] if b == NBA - 1 else (),
                        inc=(b == NBA - 1))
                emit("ve", lambda e, _e=e4: e.tensor_copy(
                    carry[:, _e * 512:(_e + 1) * 512], pa[3][:, :]),
                    rd=[("pa", 3)], wr=[("carry", e4)], deps=[tcu])
            return t_vhat

        def phase_D(i, t_vhat):
            t_vT = {}
            for b in range(NBA):
                t_vT.update({(b, hh): tk for hh, tk in transpose_block(
                    lambda _d, _b=b: qv[:, _b * D + _d * 128:_b * D + (_d + 1) * 128],
                    lambda _h, _b=b: kv[:, _b * D + _h * 1024:_b * D + (_h + 1) * 1024],
                    [("qv", b, ee) for ee in range(NED)],
                    lambda _h, _b=b: [("kv", _b, 2 * _h), ("kv", _b, 2 * _h + 1)],
                    [t_vhat[(b, ee)] for ee in range(NED)]).items()})

            t_x1 = {}
            for e4 in range(NED):
                sl = next_wat()
                t_w = dma("sp", s_wat[sl], wat[:, sl * ND * 512:(sl + 1) * ND * 512],
                          wo_b4[e4], wr=[("wat", sl)], deps=[cast_toks["o"]])
                for b in range(NBA):
                    bank = next_pa()
                    tm = None
                    for d in range(ND):
                        tm = emit("pe", lambda e, _sl=sl, _b=b, _d=d, _bk=bank: e.matmul(
                            pa[_bk][:, :],
                            kv[:, _b * D + _d * 128:_b * D + (_d + 1) * 128],
                            wat_view(_sl)[:, _d, :],
                            start=(_d == 0), stop=(_d == ND - 1)),
                            rd=[("wat", sl), ("kv", b, 0), ("kv", b, 1),
                                ("kv", b, 2), ("kv", b, 3)] if d in (0, ND - 1) else (),
                            war=[("pa", bank)] if d == 0 else (),
                            wr=[("pa", bank)] if d == ND - 1 else (),
                            deps=[t_w, t_vT[(b, 0)], t_vT[(b, 1)]] if d == 0 else (),
                            inc=(d == ND - 1))
                    t_x1[(b, e4)] = emit("ve", lambda e, _b=b, _e=e4, _bk=bank, _i=i: e.tensor_tensor(
                        xsl(_i, _b, _e), xsl(_i, _b, _e), pa[_bk][:, :], ALU.add),
                        rd=[("pa", bank)], wr=[("x", b, e4)], deps=[tm])

            t_xqT = {}
            for b in range(NBA):
                pb = b % 2
                for c in range(4):
                    emit("ve", lambda e, _b=b, _c=c, _i=i: e.bn_stats(
                        stA[:, _b * 24 + _c * 6:_b * 24 + (_c + 1) * 6],
                        xsl(_i, _b, _c)),
                        rd=[("x", b, c)], wr=[("stA", b, c)],
                        deps=[t_x1[(b, c)]])
                t_mv = emit("ve", lambda e, _b=b: e.bn_aggr(
                    mv2[:, 2 * _b:2 * _b + 2], stA[:, _b * 24:(_b + 1) * 24]),
                    rd=[("stA", b, cc) for cc in range(4)], wr=[("mv2", b)])
                t_sq = emit("act", lambda e, _b=b: e.activation(
                    r2[:, _b:_b + 1], mv2[:, 2 * _b + 1:2 * _b + 2], AF.Sqrt,
                    bias=np.float32(1e-11), scale=np.float32(1.000001)),
                    rd=[("mv2", b)], wr=[("r2", b)], deps=[t_mv])
                t_r = emit("ve", lambda e, _b=b: e.reciprocal(
                    r2[:, _b:_b + 1], r2[:, _b:_b + 1]), wr=[("r2", b)], deps=[t_sq])
                t_rg = emit("ve", lambda e, _b=b: e.tensor_single_scalar(
                    r2[:, _b:_b + 1], r2[:, _b:_b + 1], thgf[:, 3:4], ALU.mult),
                    rd=[("fcthg",)], wr=[("r2", b)], deps=[t_r])
                t_nm = emit("ve", lambda e, _b=b: e.tensor_scalar(
                    nm2[:, _b:_b + 1], mv2[:, 2 * _b:2 * _b + 1], r2[:, _b:_b + 1],
                    -1.0, ALU.mult, ALU.mult),
                    rd=[("mv2", b), ("r2", b)], wr=[("nm2", b)], deps=[t_rg])
                t_a2 = emit("act", lambda e, _b=b, _pb=pb, _i=i: e.activation(
                    xn[:, _pb * D:(_pb + 1) * D], xsl(_i, _b), AF.Identity,
                    bias=nm2[:, _b:_b + 1], scale=r2[:, _b:_b + 1]),
                    rd=[("r2", b), ("nm2", b)] + [("x", b, ee) for ee in range(NED)],
                    wr=[("xa", pb)],
                    deps=[t_nm] + [t_x1[(b, ee)] for ee in range(NED)])
                t_xqT.update({(b, hh): tk for hh, tk in transpose_block(
                    lambda _d, _pb=pb: xn[:, _pb * D + _d * 128:_pb * D + (_d + 1) * 128],
                    lambda _h, _b=b: xqT[:, _b * D + _h * 1024:_b * D + (_h + 1) * 1024],
                    [("xa", pb)], lambda _h, _b=b: [("xT", _b, _h)],
                    [t_a2]).items()})
            return t_x1, t_xqT

        def phase_E(i, sub, t_xqT, t_p0a, t_p0b):
            t_h = {}
            for e16 in range(NEF):
                sl = next_wat()
                t_w = dma("pool", s_wfc[sl], wat[:, sl * ND * 512:(sl + 1) * ND * 512],
                          wfc_b4[e16], wr=[("wat", sl)], deps=[t_p0a, t_p0b])
                for lb in range(NBF):
                    gb = sub * NBF + lb
                    bank = next_pa()
                    tm = None
                    for d in range(ND):
                        tm = emit("pe", lambda e, _sl=sl, _gb=gb, _d=d, _bk=bank: e.matmul(
                            pa[_bk][:, :],
                            xqT[:, _gb * D + _d * 128:_gb * D + (_d + 1) * 128],
                            wat_view(_sl)[:, _d, :],
                            start=(_d == 0), stop=(_d == ND - 1)),
                            rd=[("wat", sl), ("xT", gb, 0), ("xT", gb, 1)] if d in (0, ND - 1) else (),
                            war=[("pa", bank)] if d == 0 else (),
                            wr=[("pa", bank)] if d == ND - 1 else (),
                            deps=[t_w, t_xqT[(gb, 0)], t_xqT[(gb, 1)]] if d == 0 else (),
                            inc=(d == ND - 1))
                    t_h[(lb, e16)] = emit("act", lambda e, _lb=lb, _e=e16, _bk=bank: e.activation(
                        h[:, _lb * FF + _e * 512:_lb * FF + (_e + 1) * 512],
                        pa[_bk][:, :], AF.Gelu),
                        rd=[("pa", bank)], wr=[("h", lb, e16)], deps=[tm])

            t_a3 = {}
            for lb in range(NBF):
                for c in range(16):
                    emit("ve", lambda e, _lb=lb, _c=c: e.bn_stats(
                        stH[:, _lb * 96 + _c * 6:_lb * 96 + (_c + 1) * 6],
                        h[:, _lb * FF + _c * 512:_lb * FF + (_c + 1) * 512]),
                        rd=[("h", lb, c)], wr=[("stH", lb, c)], deps=[t_h[(lb, c)]])
                t_mv = emit("ve", lambda e, _lb=lb: e.bn_aggr(
                    mv3[:, 2 * _lb:2 * _lb + 2], stH[:, _lb * 96:(_lb + 1) * 96]),
                    rd=[("stH", lb, cc) for cc in range(16)], wr=[("mv3", lb)])
                t_sq = emit("act", lambda e, _lb=lb: e.activation(
                    r3[:, _lb:_lb + 1], mv3[:, 2 * _lb + 1:2 * _lb + 2], AF.Sqrt,
                    bias=np.float32(1e-6)),
                    rd=[("mv3", lb)], wr=[("r3", lb)], deps=[t_mv])
                t_r = emit("ve", lambda e, _lb=lb: e.reciprocal(
                    r3[:, _lb:_lb + 1], r3[:, _lb:_lb + 1]), wr=[("r3", lb)], deps=[t_sq])
                t_rg = emit("ve", lambda e, _lb=lb: e.tensor_single_scalar(
                    r3[:, _lb:_lb + 1], r3[:, _lb:_lb + 1], thgp[:, 3:4], ALU.mult),
                    rd=[("pjthg",)], wr=[("r3", lb)], deps=[t_r])
                t_nm = emit("ve", lambda e, _lb=lb: e.tensor_scalar(
                    nm3[:, _lb:_lb + 1], mv3[:, 2 * _lb:2 * _lb + 1], r3[:, _lb:_lb + 1],
                    -1.0, ALU.mult, ALU.mult),
                    rd=[("mv3", lb), ("r3", lb)], wr=[("nm3", lb)], deps=[t_rg])
                qs = []
                for qq in range(4):
                    qs.append(emit("ve", lambda e, _lb=lb, _q=qq: e.tensor_scalar(
                        h[:, _lb * FF + _q * 2048:_lb * FF + (_q + 1) * 2048],
                        h[:, _lb * FF + _q * 2048:_lb * FF + (_q + 1) * 2048],
                        r3[:, _lb:_lb + 1], nm3[:, _lb:_lb + 1], ALU.mult, ALU.add),
                        rd=[("r3", lb), ("nm3", lb)] + [("h", lb, cc) for cc in range(4 * qq, 4 * qq + 4)],
                        wr=[("h", lb, cc) for cc in range(4 * qq, 4 * qq + 4)],
                        deps=[t_nm] + [t_h[(lb, cc)] for cc in range(4 * qq, 4 * qq + 4)]))
                t_a3[lb] = qs
            return t_a3

        def phase_F(i, sub, t_a3, t_p0a, t_p0b):
            x2_toks = {lb: [None] * NED for lb in range(NBF)}
            for pas in range(2):
                stop_toks = {}
                for g in range(8):
                    gp = g % 2
                    t_hqT = {}
                    for lb in range(NBF):
                        tt = None
                        for d8 in range(8):
                            tt = emit("pe", lambda e, _lb=lb, _g=g, _d8=d8: e.transpose(
                                pt[_lb][:, _d8 * 128:(_d8 + 1) * 128],
                                h[:, _lb * FF + _g * 1024 + _d8 * 128:
                                  _lb * FF + _g * 1024 + (_d8 + 1) * 128], c_id[:]),
                                rd=[("h", lb, (g * 1024 + d8 * 128) // 512)] if d8 in (0, 7) else (),
                                war=[("pt", lb)] if d8 == 0 else (),
                                wr=[("pt", lb)] if d8 == 7 else (),
                                deps=[t_a3[lb][g // 2]] if d8 == 0 else (),
                                inc=(d8 == 7))
                        t_hqT[lb] = emit("ve", lambda e, _lb=lb, _gp=gp: e.tensor_copy(
                            hqT[:, (_lb * 2 + _gp) * 1024:(_lb * 2 + _gp + 1) * 1024],
                            pt[_lb][:, :]),
                            rd=[("pt", lb)], wr=[("hqT", lb, gp)], deps=[tt])
                    for half4 in range(2):
                        g4 = g * 2 + half4
                        wsl = g4 % 2
                        t_w = dma("pool", s_wpj[wsl], wpj[:, wsl * 4096:(wsl + 1) * 4096],
                                  wpj_b4[pas, g4], wr=[("wpj", wsl)],
                                  deps=[t_p0a, t_p0b])
                        for dd4 in range(4):
                            dd = half4 * 4 + dd4
                            dprime = g * 8 + dd
                            for lb in range(NBF):
                                for ep in range(2):
                                    bank = lb * 2 + ep
                                    last = (ep == 1)
                                    tok = emit("pe", lambda e, _lb=lb, _gp=gp, _dd=dd, _w=wsl, _d4=dd4, _ep=ep, _dp=dprime, _bk=bank: e.matmul(
                                        pa[_bk][:, :],
                                        hqT[:, (_lb * 2 + _gp) * 1024 + _dd * 128:
                                            (_lb * 2 + _gp) * 1024 + (_dd + 1) * 128],
                                        wpj[:, _w * 4096 + _d4 * 1024 + _ep * 512:
                                            _w * 4096 + _d4 * 1024 + (_ep + 1) * 512],
                                        start=(_dp == 0), stop=(_dp == NF - 1)),
                                        rd=[("hqT", lb, gp), ("wpj", wsl)] if last else (),
                                        war=[("pa", bank)] if dprime == 0 else (),
                                        wr=[("pa", bank)] if dprime == NF - 1 else (),
                                        deps=[t_hqT[lb], t_w] if ep == 0 else (),
                                        inc=(last or dprime == NF - 1))
                                    if dprime == NF - 1:
                                        stop_toks[(lb, ep)] = tok
                for lb in range(NBF):
                    gb = sub * NBF + lb
                    for ep in range(2):
                        e4 = pas * 2 + ep
                        bank = lb * 2 + ep
                        x2_toks[lb][e4] = emit("ve", lambda e, _gb=gb, _e=e4, _bk=bank, _i=i: e.tensor_tensor(
                            xsl(_i, _gb, _e), xsl(_i, _gb, _e), pa[_bk][:, :], ALU.add),
                            rd=[("pa", bank)], wr=[("x", gb, e4)],
                            deps=[stop_toks[(lb, ep)]])
            for lb in range(NBF):
                gb = sub * NBF + lb
                dma("act", s_out[gb % 2], out[(i * NBA + gb) * 128:(i * NBA + gb + 1) * 128, :],
                    xsl(i, gb), rd=[("x", gb, ee) for ee in range(NED)],
                    wr=[("xf", gb)], deps=x2_toks[lb])

        def dbg_post_B(i, t_q, t_kv):
            if not dbg:
                return
            for b in range(NBA):
                rows = slice((i * NBA + b) * 128, (i * NBA + b + 1) * 128)
                dma("sp", s_dbg, dbg_q[rows, :], qv[:, b * D:(b + 1) * D],
                    rd=[("qv", b, ee) for ee in range(NED)],
                    deps=[t_q[(b, ee)] for ee in range(NED)])
                dma("sp", s_dbg, dbg_kv[rows, :], kv[:, b * D:(b + 1) * D],
                    rd=[("kv", b, ee) for ee in range(NED)],
                    deps=[t_kv[(b, ee)] for ee in range(NED)])

        def dbg_post_C(i, t_vhat):
            if not dbg:
                return
            for b in range(NBA):
                rows = slice((i * NBA + b) * 128, (i * NBA + b + 1) * 128)
                dma("sp", s_dbg, dbg_vh[rows, :], qv[:, b * D:(b + 1) * D],
                    rd=[("qv", b, ee) for ee in range(NED)],
                    deps=[t_vhat[(b, ee)] for ee in range(NED)])

        def dbg_post_x1(i, t_x1):
            if not dbg:
                return
            for b in range(NBA):
                rows = slice((i * NBA + b) * 128, (i * NBA + b + 1) * 128)
                dma("sp", s_dbg, dbg_x1[rows, :], xsl(i, b),
                    rd=[("x", b, ee) for ee in range(NED)],
                    deps=[t_x1[(b, ee)] for ee in range(NED)])

        def dbg_post_E(i, sub, t_a3):
            if not dbg:
                return
            for lb in range(NBF):
                gb = sub * NBF + lb
                rows = slice((i * NBA + gb) * 128, (i * NBA + gb + 1) * 128)
                dma("sp", s_dbg, dbg_h[rows, :], h[:, lb * FF:(lb + 1) * FF],
                    rd=[("h", lb, cc) for cc in range(16)], deps=list(t_a3[lb]))

        p0toks = [None, None]

        def set_p0toks():
            p0toks[0] = (s_p0o[0], s_p0o[0].count)
            p0toks[1] = (s_p0o[1], s_p0o[1].count)

        t_xin = tile_load(0)
        t_xnT = phase_A(0, t_xin)
        t_q, t_kv = phase_B(0, t_xnT)
        dbg_post_B(0, t_q, t_kv)
        t_vhat = phase_C(0, t_q, t_kv)
        dbg_post_C(0, t_vhat)
        phase0_matrix(wfct, D, FF, thgf, "fc", fc_outdst)
        fctoks = [(s_p0o[0], s_p0o[0].count), (s_p0o[1], s_p0o[1].count)]
        t_x1, t_xqT = phase_D(0, t_vhat)
        dbg_post_x1(0, t_x1)
        phase0_matrix(wpjt, FF, D, thgp, "pj", pj_outdst)
        set_p0toks()
        for sub in range(NSUB):
            t_a3 = phase_E(0, sub, t_xqT, fctoks[0], fctoks[1])
            dbg_post_E(0, sub, t_a3)
            phase_F(0, sub, t_a3, p0toks[0], p0toks[1])

        for i in range(1, NT):
            t_xin = tile_load(i)
            t_xnT = phase_A(i, t_xin)
            t_q, t_kv = phase_B(i, t_xnT)
            dbg_post_B(i, t_q, t_kv)
            t_vhat = phase_C(i, t_q, t_kv)
            dbg_post_C(i, t_vhat)
            t_x1, t_xqT = phase_D(i, t_vhat)
            dbg_post_x1(i, t_x1)
            for sub in range(NSUB):
                t_a3 = phase_E(i, sub, t_xqT, p0toks[0], p0toks[1])
                dbg_post_E(i, sub, t_a3)
                phase_F(i, sub, t_a3, p0toks[0], p0toks[1])

        for so in s_out:
            if so.count:
                bd.raw_wait("act", (so, so.count))

        with nc.Block() as block:
            block.tensor(lambda eng: [op(eng) for op in bd.st["pe"].ops])
            block.vector(lambda eng: [op(eng) for op in bd.st["ve"].ops])
            block.scalar(lambda eng: [op(eng) for op in bd.st["act"].ops])
            block.gpsimd(lambda eng: [op(eng) for op in bd.st["pool"].ops])
            block.sync(lambda eng: [op(eng) for op in bd.st["sp"].ops])

    return nc


_CACHE = {}


def _get_program(S, T):
    key = (S, T)
    if key not in _CACHE:
        _CACHE[key] = build_program(S, T)
    return _CACHE[key]


def kernel(L0_attn_x, L0_attn_Wq, L0_attn_Wk, L0_attn_Wv, L0_attn_Wo,
           L0_ffn_w_fc, L0_ffn_w_proj):
    x = np.ascontiguousarray(np.asarray(L0_attn_x))
    B, S, _ = x.shape
    ins = {
        "wqt": np.ascontiguousarray(np.asarray(L0_attn_Wq).T),
        "wkt": np.ascontiguousarray(np.asarray(L0_attn_Wk).T),
        "wvt": np.ascontiguousarray(np.asarray(L0_attn_Wv).T),
        "wot": np.ascontiguousarray(np.asarray(L0_attn_Wo).T),
        "wfct": np.ascontiguousarray(np.asarray(L0_ffn_w_fc).T),
        "wpjt": np.ascontiguousarray(np.asarray(L0_ffn_w_proj).T),
    }
    nc = _get_program(S, 512)
    in_maps = [dict(ins, x=np.ascontiguousarray(x[b])) for b in range(B)]
    res = run_bass_kernel_spmd(nc, in_maps, core_ids=list(range(B)))
    return np.stack([res.results[b]["out"] for b in range(B)], axis=0)
